# version 05 (final)
# baseline (speedup 1.0000x reference)
import numpy as np
import concourse.bass as bass
import concourse.mybir as mybir
from concourse.bass_utils import run_bass_kernel_spmd
from concourse.alu_op_type import AluOpType
from contextlib import ExitStack

F32 = mybir.dt.float32
BF16 = mybir.dt.bfloat16
AF = mybir.ActivationFunctionType
AX = mybir.AxisListType
MUL = AluOpType.mult
ADD = AluOpType.add
SUB = AluOpType.subtract
EQU = AluOpType.is_equal
MAX = AluOpType.max

D = 1024
T = 4096
TQ = 1024
FF = 2048
NE = 9
NH = 16

W13_E = 16 * 128 * 2048
W2_E = 128 * 16384


class Ctr:

    def __init__(self, nc, st, name):
        self.h = st.enter_context(nc.semaphore(name))
        self.n = 0

    def inc(self, inst, k=1):
        inst.then_inc(self.h, k)
        self.n += k
        return inst

    def wait(self, eng, val=None):
        v = self.n if val is None else val
        if v > 0:
            eng.wait_ge(self.h, v)


def emit_prelude(nc, g, S, d):
    sC, sV = S["sC"], S["sV"]
    for dst, src in (
        (g["identf"][:, :], d["idf"][:, :]),
        (g["dec_s"][:, :], d["decc"][:, :]),
        (g["eps_s"][:, :], d["epsc"][:, :]),
        (g["wg_s"][:, :], d["wg8"][:, :]),
        (g["bq8"][:, :], d["bq8"][:, :]),
        (g["bo8"][:, :], d["bo8"][:, :]),
        (g["identb"][:, :], d["idf"][:, :]),
        (g["bkrow"][0:1, :], d["bkr"][0:1, :]),
        (g["bvrow"][0:1, :], d["bvr"][0:1, :]),
    ):
        sC.inc(nc.gpsimd.dma_start(dst, src), 16)
    nc.vector.memset(g["onesc_f"][:, :], 1.0)
    nc.vector.memset(g["onesc_b"][:, :], 1.0)
    nc.vector.memset(g["onesr_f"][:, :], 1.0)
    nc.vector.memset(g["onesr_b"][0:1, :], 1.0)
    sV.inc(nc.vector.memset(g["epsln"][:, :], 1e-5))
    sC.wait(nc.tensor)
    sV.wait(nc.tensor)
    for s in range(8):
        mm = nc.tensor.matmul(g["pA"][0:1, 0:8], g["onesc_f"][:, :],
                              g["wg_s"][:, 8 * s:8 * s + 8],
                              start=(s == 0), stop=(s == 7))
    S["sM"].inc(mm)
    S["sM"].wait(nc.vector)
    sV.inc(nc.vector.tensor_copy(g["rowtmp"][0:1, 0:8], g["pA"][0:1, 0:8]))
    sV.wait(nc.tensor)
    S["sM"].inc(nc.tensor.matmul(g["pB"][:, 0:8], g["onesr_f"][0:1, :],
                                 g["rowtmp"][0:1, 0:8], start=True, stop=True))
    S["sM"].wait(nc.vector)
    sV.inc(nc.vector.tensor_copy(g["cg8"][:, :], g["pB"][:, 0:8]))
    nc.all_engine_barrier()


def emit_prep(nc, g, S, d):
    sPC, sPP, sPD = S["sPC"], S["sPP"], S["sPD"]
    CH = 128 * 2048
    for e in range(NE):
        for sub in range(24):
            if sub < 16:
                src, soff = d["w13r"], e * W13_E + sub * CH
                dst, doff = d["sw13"], e * W13_E + sub * CH
            else:
                src, soff = d["w2r"], e * W2_E + (sub - 16) * CH
                dst, doff = d["sw2"], e * W2_E + (sub - 16) * CH
            i = e * 24 + sub
            if i >= 1:
                sPP.wait(nc.gpsimd)
                sPD.inc(nc.gpsimd.nop())
            sPC.inc(nc.gpsimd.dma_start(
                bass.AP(g["wstage"][i % 2], 0, [[2048, 128], [1, 2048]]),
                bass.AP(src, soff, [[2048, 128], [1, 2048]])), 16)
            sPC.wait(nc.gpsimd)
            sPP.inc(nc.gpsimd.dma_start(
                bass.AP(dst, doff, [[2048, 128], [1, 2048]]),
                bass.AP(g["wstage"][i % 2], 0, [[2048, 128], [1, 2048]])), 16)
    sPP.wait(nc.gpsimd)
    sPD.inc(nc.gpsimd.nop())


def alloc_ffn(nc, st, tag):
    def sbf(name, shape, dt):
        return st.enter_context(nc.sbuf_tensor(tag + name, shape, dt))
    g = {}
    g["xq"] = [sbf(f"xq{s}", [128, TQ], F32) for s in range(8)]
    g["xh"] = [sbf(f"xh{s}", [128, TQ], BF16) for s in range(8)]
    g["acc"] = [sbf(f"acc{s}", [128, TQ], F32) for s in range(8)]
    g["hb"] = [sbf(f"hb{f}", [128, TQ], BF16) for f in range(16)]
    g["sq"] = sbf("sq", [128, TQ], F32)
    g["sil"] = [sbf(f"sil{p}", [128, 512], BF16) for p in range(2)]
    g["tmp2"] = [sbf(f"tmp2{p}", [128, 512], F32) for p in range(2)]
    g["wu"] = [sbf(f"wu{p}", [128, 2048], BF16) for p in range(2)]
    g["wdt"] = sbf("wdt", [128, 16384], BF16)
    g["wbc"] = sbf("wbc", [128, TQ], F32)
    g["w8t"] = sbf("w8t", [8, TQ], F32)
    g["rowsS"] = sbf("rowsS", [33, TQ], F32)
    g["rowM"] = sbf("rowM", [1, TQ], F32)
    g["rmcol"] = sbf("rmcol", [128, 16], F32)
    g["glog"] = sbf("glog", [128, 64], F32)
    g["rbc"] = sbf("rbc", [128, TQ], F32)
    g["mrbc"] = sbf("mrbc", [128, TQ], F32)
    g["wrow"] = g["rowM"]
    for nm in ["gA", "gB", "gt", "eq1", "eq2", "gm", "wa", "wb_", "wtok"]:
        g[nm] = sbf(nm, [128, 8], F32)
    for nm in ["m1", "m2", "v1", "v2", "rs", "rs2"]:
        g[nm] = sbf(nm, [128, 1], F32)
    g["pu"] = [st.enter_context(nc.psum_tensor(tag + f"pu{i}", [128, 512],
                                               F32)) for i in range(4)]
    g["pd"] = [st.enter_context(nc.psum_tensor(tag + f"pd{i}", [128, 512],
                                               F32)) for i in range(2)]
    g["pw"] = [st.enter_context(nc.psum_tensor(tag + f"pw{i}", [128, 512],
                                               F32)) for i in range(2)]
    return g


def emit_ffn(nc, g, gl, S, d, xin, xout, first, experts=NE, quarters=4):
    (sX, sSQ, sST, sGL, sRW, sTR, sRC, sGT, sSG, sGW, sBC, sBE, sXH, sGR,
     sMW, sWB, sWD, sM1, sMU, sSI, sEV, sMD, sVD, sVO) = (
        S[k] for k in [
            "sX", "sSQ", "sST", "sGL", "sRW", "sTR", "sRC", "sGT", "sSG",
            "sGW", "sBC", "sBE", "sXH", "sGR", "sMW", "sWB", "sWD",
            "sM1", "sMU", "sSI", "sEV", "sMD", "sVD", "sVO"])
    xq, xh, acc, hb = g["xq"], g["xh"], g["acc"], g["hb"]
    sq, sil, tmp2 = g["sq"], g["sil"], g["tmp2"]
    wu, wdt, wbc, wrow, w8t = g["wu"], g["wdt"], g["wbc"], g["wrow"], g["w8t"]
    rowsS, rowM, rmcol = g["rowsS"], g["rowM"], g["rmcol"]
    rbc, mrbc = g["rbc"], g["mrbc"]
    pu, pd, pw = g["pu"], g["pd"], g["pw"]

    for q in range(quarters):
        for s in range(8):
            sVO.wait(nc.scalar, sVO.n - 14 + 2 * s)
            sX.inc(nc.scalar.dma_start(
                xq[s][:, :],
                bass.AP(xin, s * 128 * T + q * TQ, [[T, 128], [1, TQ]])), 16)
        st_base = sST.n
        for s in range(8):
            if s == 0:
                sX.wait(nc.scalar)
                sRW.wait(nc.scalar, sRW.n)
            sST.wait(nc.scalar, st_base if s == 0 else st_base + s)
            sSQ.inc(nc.scalar.activation(sq[:, :], xq[s][:, :], AF.Square))
        for s in range(8):
            sSQ.wait(nc.tensor, sSQ.n - (7 - s))
            if s == 0:
                sSI.wait(nc.tensor)
                sEV.wait(nc.tensor)
            for h in range(2):
                nc.tensor.matmul(pu[h][0:1, :], gl["onesc_f"][:, :],
                                 xq[s][:, 512 * h:512 * (h + 1)],
                                 start=(s == 0), stop=(s == 7))
                mm = nc.tensor.matmul(pu[h][32:33, :], gl["onesc_f"][:, :],
                                      sq[:, 512 * h:512 * (h + 1)],
                                      start=(s == 0), stop=(s == 7))
            sST.inc(mm)
        for j in range(8):
            for s in range(8):
                mm = nc.tensor.matmul(pu[2][:, 8 * j:8 * j + 8],
                                      xq[s][:, 128 * j:128 * (j + 1)],
                                      gl["wg_s"][:, 8 * s:8 * s + 8],
                                      start=(s == 0), stop=(s == 7))
            sGL.inc(mm)
        sGL.wait(nc.vector)
        S["sGE"].inc(nc.vector.tensor_copy(g["glog"][:, :], pu[2][:, 0:64]))
        sST.wait(nc.vector)
        sMW.wait(nc.vector, sMW.n)
        sBC.wait(nc.vector, sBC.n)
        for h in range(2):
            nc.vector.tensor_scalar(rowM[0:1, 512 * h:512 * (h + 1)],
                                    pu[h][0:1, :], 1.0 / D, None, MUL)
            sRW.inc(nc.vector.tensor_scalar(sq[0:1, 512 * h:512 * (h + 1)],
                                            pu[h][32:33, :], 1.0 / D, None,
                                            MUL))
        sRW.wait(nc.vector)
        sRW.inc(nc.vector.tensor_tensor(rbc[0:1, :], rowM[0:1, :],
                                        rowM[0:1, :], MUL))
        sRW.wait(nc.vector)
        sRW.inc(nc.vector.tensor_tensor(sq[0:1, :], sq[0:1, :],
                                        rbc[0:1, :], SUB))
        sRW.wait(nc.scalar)
        sRW.inc(nc.scalar.activation(sq[0:1, :], sq[0:1, :], AF.Sqrt,
                                     bias=gl["epsln"][0:1, 0:1]))
        sRW.wait(nc.vector)
        sRW.inc(nc.vector.reciprocal(rowsS[0:1, :], sq[0:1, :]))
        sRW.wait(nc.vector)
        sRW.inc(nc.vector.tensor_tensor(rowsS[32:33, :], rowM[0:1, :],
                                        rowsS[0:1, :], MUL))
        sRW.wait(nc.tensor)
        tr_base, rc_base = sTR.n, sRC.n
        for j in range(8):
            if j >= 2:
                sRC.wait(nc.tensor, sRC.n - 1)
            else:
                sWB.wait(nc.tensor)
            sTR.inc(nc.tensor.transpose(pw[j % 2][:, 0:33],
                                        rowsS[0:33, 128 * j:128 * (j + 1)],
                                        gl["identf"][0:33, 0:33]))
            sTR.wait(nc.vector, tr_base + j + 1)
            nc.vector.tensor_copy(rmcol[:, 2 * j:2 * j + 1],
                                  pw[j % 2][:, 0:1])
            sRC.inc(nc.vector.tensor_copy(rmcol[:, 2 * j + 1:2 * j + 2],
                                          pw[j % 2][:, 32:33]))
        bc_base = sBC.n
        for rnd in range(4):
            src = (rowsS[0:1, 512 * (rnd % 2):512 * (rnd % 2 + 1)] if rnd < 2
                   else rowsS[32:33, 512 * (rnd % 2):512 * (rnd % 2 + 1)])
            if rnd < 2:
                sRC.wait(nc.tensor)
            else:
                sBE.wait(nc.tensor, sBE.n - 1)
            ones_row = (gl["onesr_f"][0:1, :] if rnd < 2 else
                        gl["onesr_f"][32:33, :])
            sBC.inc(nc.tensor.matmul(pw[rnd % 2][:, :], ones_row,
                                     src, start=True, stop=True))
            sBC.wait(nc.vector, bc_base + rnd + 1)
            dst = (rbc if rnd < 2 else mrbc)[:, 512 * (rnd % 2):
                                             512 * (rnd % 2 + 1)]
            sBE.inc(nc.vector.tensor_copy(dst, pw[rnd % 2][:, :]))
        xh_base = sXH.n
        for s in range(8):
            if s == 0:
                S["sXOa"].wait(nc.vector)
                S["sXOb"].wait(nc.vector)
                S["sVA"].wait(nc.vector)
            sMU.wait(nc.vector)
            for h in range(2):
                cs = slice(512 * h, 512 * (h + 1))
                sXH.inc(nc.vector.tensor_tensor(tmp2[h][:, :], xq[s][:, cs],
                                                rbc[:, cs], MUL))
                sXH.wait(nc.vector)
                sXH.inc(nc.vector.tensor_tensor(xh[s][:, cs], tmp2[h][:, :],
                                                mrbc[:, cs], SUB))
        gw_base = sGW.n
        for j in range(8):
            sRC.wait(nc.vector)
            nc.vector.tensor_scalar(g["gA"][:, :],
                                    g["glog"][:, 8 * j:8 * j + 8],
                                    rmcol[:, 2 * j:2 * j + 1], None, MUL)
            sGT.inc(nc.vector.tensor_scalar(
                g["gB"][:, :], gl["cg8"][:, :],
                rmcol[:, 2 * j + 1:2 * j + 2], None, MUL))
            sGT.wait(nc.vector)
            sGT.inc(nc.vector.tensor_tensor(g["gt"][:, :], g["gA"][:, :],
                                            g["gB"][:, :], SUB))
            sGT.wait(nc.vector)
            sGT.inc(nc.vector.tensor_reduce(g["m1"][:, :], g["gt"][:, :],
                                            AX.X, MAX))
            sGT.wait(nc.vector)
            sGT.inc(nc.vector.tensor_scalar(g["eq1"][:, :], g["gt"][:, :],
                                            g["m1"][:, 0:1], None, EQU))
            sGT.wait(nc.vector)
            sGT.inc(nc.vector.scalar_tensor_tensor(
                g["gm"][:, :], g["eq1"][:, :], -1e30, g["gt"][:, :], MUL, ADD))
            sGT.wait(nc.vector)
            sGT.inc(nc.vector.tensor_reduce(g["m2"][:, :], g["gm"][:, :],
                                            AX.X, MAX))
            sGT.wait(nc.vector)
            sGT.inc(nc.vector.tensor_scalar(g["eq2"][:, :], g["gm"][:, :],
                                            g["m2"][:, 0:1], None, EQU))
            sGT.wait(nc.scalar)
            nc.scalar.activation(g["v1"][:, :], g["m1"][:, :], AF.Sigmoid)
            sSG.inc(nc.scalar.activation(g["v2"][:, :], g["m2"][:, :],
                                         AF.Sigmoid))
            sSG.wait(nc.vector)
            sGT.inc(nc.vector.tensor_tensor(g["rs2"][:, :], g["v1"][:, :],
                                            g["v2"][:, :], ADD))
            sGT.wait(nc.vector)
            sGT.inc(nc.vector.reciprocal(g["rs"][:, :], g["rs2"][:, :]))
            sGT.wait(nc.vector)
            nc.vector.tensor_scalar(g["wa"][:, :], g["eq1"][:, :],
                                    g["v1"][:, 0:1], g["rs"][:, 0:1],
                                    MUL, MUL)
            sGT.inc(nc.vector.tensor_scalar(g["wb_"][:, :], g["eq2"][:, :],
                                            g["v2"][:, 0:1], g["rs"][:, 0:1],
                                            MUL, MUL))
            sGT.wait(nc.vector)
            if j == 0:
                sGR.wait(nc.vector, sGR.n)
            sGT.inc(nc.vector.tensor_tensor(g["wtok"][:, :], g["wa"][:, :],
                                            g["wb_"][:, :], ADD))
            sGT.wait(nc.tensor)
            if j >= 2:
                sGW.wait(nc.tensor, sGW.n - 1)
            else:
                sVD.wait(nc.tensor, sVD.n)
            sTR.inc(nc.tensor.transpose(pd[j % 2][0:8, 0:128],
                                        g["wtok"][:, :], gl["identf"][:, :]))
            sTR.wait(nc.vector)
            sGW.inc(nc.vector.tensor_copy(w8t[0:8, 128 * j:128 * (j + 1)],
                                          pd[j % 2][0:8, 0:128]))

        for e in range(experts):
            e_glob = (S["sWUa"].n + S["sWUb"].n) // 256
            mu_base = sMU.n
            if e < 8:
                sMW.wait(nc.sync, sMW.n)
                sGW.wait(nc.sync, gw_base + 8)
                sGR.inc(nc.sync.dma_start(wrow[0:1, :], w8t[e:e + 1, :]), 16)

            wu_done = {}

            def wu_load(f):
                if first:
                    S["sPD"].wait(nc.sync, min(e_glob * 24 + f + 1, 24 * NE))
                sMU.wait(nc.sync, mu_base + 2 * f - 2)
                sWUp = S["sWUa"] if f % 2 == 0 else S["sWUb"]
                sWUp.inc(nc.sync.dma_start(
                    bass.AP(wu[f % 2], 0, [[2048, 128], [1, 2048]]),
                    bass.AP(d["sw13"], (e * 16 + f) * 128 * 2048,
                            [[2048, 128], [1, 2048]])), 16)
                wu_done[f] = (sWUp, sWUp.n)
            wu_load(0)
            wu_load(1)
            if first:
                S["sPD"].wait(nc.sync, min(e_glob * 24 + 24, 24 * NE))
            sMD.wait(nc.sync, sMD.n)
            for i in range(4):
                sWD.inc(nc.sync.dma_start(
                    wdt[:, 4096 * i:4096 * (i + 1)],
                    bass.AP(d["sw2"], e * W2_E + 4096 * i,
                            [[16384, 128], [1, 4096]])), 16)
            sWD.wait(nc.sync)
            S["sWDR"].inc(nc.sync.nop())
            for f in range(2, 16):
                wu_load(f)

            ev_base, si_base = sEV.n, sSI.n
            md_prev = sMD.n
            for f in range(16):
                for half in range(2):
                    u = 2 * f + half
                    wu_done[f][0].wait(nc.tensor, wu_done[f][1])
                    if e == 0 and u == 0:
                        sXH.wait(nc.tensor, xh_base + 32)
                    if e == 0 and u == 1:
                        S["sGE"].wait(nc.tensor)
                    if u >= 2:
                        sEV.wait(nc.tensor, ev_base + u - 1)
                        sSI.wait(nc.tensor, si_base + u - 1)
                    bk = 2 * (u % 2)
                    for s in range(8):
                        mm = nc.tensor.matmul(
                            pu[bk][:, :],
                            wu[f % 2][:, 128 * s:128 * (s + 1)],
                            xh[s][:, 512 * half:512 * (half + 1)],
                            start=(s == 0), stop=(s == 7))
                    sM1.inc(mm)
                    for s in range(8):
                        mm = nc.tensor.matmul(
                            pu[bk + 1][:, :],
                            wu[f % 2][:, 1024 + 128 * s:
                                      1024 + 128 * (s + 1)],
                            xh[s][:, 512 * half:512 * (half + 1)],
                            start=(s == 0), stop=(s == 7))
                    sMU.inc(mm)
                    sM1.wait(nc.scalar)
                    if u >= 2:
                        sEV.wait(nc.scalar, ev_base + u - 1)
                    sSI.inc(nc.scalar.activation(sil[u % 2][:, :],
                                                 pu[bk][:, :], AF.Silu))
                    sSI.wait(nc.vector)
                    sMU.wait(nc.vector)
                    sMD.wait(nc.vector, md_prev - 8 * (1 - half))
                    sEV.inc(nc.vector.tensor_tensor(
                        hb[f][:, 512 * half:512 * (half + 1)],
                        sil[u % 2][:, :], pu[bk + 1][:, :], MUL))

            if e < 8:
                sGR.wait(nc.tensor)
                if e > 0:
                    sWB.wait(nc.tensor)
                else:
                    sBE.wait(nc.tensor)
                for h in range(2):
                    sMW.inc(nc.tensor.matmul(
                        pw[h][:, :], gl["onesr_f"][0:1, :],
                        wrow[0:1, 512 * h:512 * (h + 1)],
                        start=True, stop=True))
                sMW.wait(nc.vector)
                sVD.wait(nc.vector, sVD.n)
                for h in range(2):
                    w_inst = nc.vector.tensor_copy(
                        wbc[:, 512 * h:512 * (h + 1)], pw[h][:, :])
                    if h == 1:
                        sWB.inc(w_inst)

            S["sWDR"].wait(nc.tensor)
            vd_base = sVD.n
            pend = None
            for dtile in range(16):
                half, dch = dtile // 8, dtile % 8
                sEV.wait(nc.tensor, ev_base + 31 + half)
                if e == 0 and dtile < 2:
                    sGW.wait(nc.tensor, gw_base + 8)
                sVD.wait(nc.tensor, vd_base + dtile - 1)
                for f in range(16):
                    mm = nc.tensor.matmul(
                        pd[dtile % 2][:, :],
                        wdt[:, 1024 * f + 128 * dch:1024 * f + 128 * (dch + 1)],
                        hb[f][:, 512 * half:512 * (half + 1)],
                        start=(f == 0), stop=(f == 15))
                sMD.inc(mm)
                sMD.wait(nc.vector)
                sVA = S["sVA"]
                dst = acc[dch][:, 512 * half:512 * (half + 1)]
                if e == 0:
                    sWB.wait(nc.vector)
                    sVO.wait(nc.vector)
                    sVD.inc(nc.vector.tensor_tensor(
                        dst, pd[dtile % 2][:, :],
                        wbc[:, 512 * half:512 * (half + 1)], MUL))
                elif e == 8:
                    sVA.wait(nc.vector)
                    sVD.inc(nc.vector.tensor_tensor(
                        dst, dst, pd[dtile % 2][:, :], ADD))
                else:
                    sWB.wait(nc.vector)
                    sVA.wait(nc.vector)
                    sVD.inc(nc.vector.tensor_tensor(
                        tmp2[dtile % 2][:, :], pd[dtile % 2][:, :],
                        wbc[:, 512 * half:512 * (half + 1)], MUL))
                    if pend is not None:
                        sVD.wait(nc.vector, sVD.n - 1)
                        sVA.inc(nc.vector.tensor_tensor(
                            pend[0], pend[0], tmp2[pend[1]][:, :], ADD))
                    pend = (dst, dtile % 2)
            if pend is not None:
                sVD.wait(nc.vector, sVD.n)
                sVA.inc(nc.vector.tensor_tensor(
                    pend[0], pend[0], tmp2[pend[1]][:, :], ADD))
                pend = None

        sVD.wait(nc.vector)
        S["sVA"].wait(nc.vector)
        for s in range(8):
            for h in range(2):
                k = 2 * s + h
                cs = slice(512 * h, 512 * (h + 1))
                sXOp = S["sXOa"] if h == 0 else S["sXOb"]
                if k >= 2:
                    sXOp.wait(nc.vector)
                sVO.inc(nc.vector.scalar_tensor_tensor(
                    tmp2[h][:, :], acc[s][:, cs], 1.0 / 6.0, xq[s][:, cs],
                    MUL, ADD))
                sVO.wait(nc.scalar)
                sXOp.inc(nc.scalar.dma_start(
                    bass.AP(xout, s * 128 * T + q * TQ + 512 * h,
                            [[T, 128], [1, 512]]),
                    tmp2[h][:, :]), 16)
    S["sXOa"].wait(nc.scalar)
    S["sXOb"].wait(nc.scalar)
    nc.all_engine_barrier()


def build_nc(parts=("ffn1", "attn", "ffn2")):
    nc = bass.Bass("TRN2", target_bir_lowering=False)
    d = {}
    d["x0"] = nc.dram_tensor("x0", [D, T], F32, kind="ExternalInput")
    for nm in ["wqt", "wkt", "wvt", "wot"]:
        d[nm] = nc.dram_tensor(nm, [D, D], F32, kind="ExternalInput")
    d["bq8"] = nc.dram_tensor("bq8", [128, 8], F32, kind="ExternalInput")
    d["bo8"] = nc.dram_tensor("bo8", [128, 8], F32, kind="ExternalInput")
    d["bkr"] = nc.dram_tensor("bkr", [1, D], F32, kind="ExternalInput")
    d["bvr"] = nc.dram_tensor("bvr", [1, D], F32, kind="ExternalInput")
    d["wg8"] = nc.dram_tensor("wg8", [128, 64], F32, kind="ExternalInput")
    d["w13r"] = nc.dram_tensor("w13r", [NE * W13_E], F32, kind="ExternalInput")
    d["w2r"] = nc.dram_tensor("w2r", [NE * W2_E], F32, kind="ExternalInput")
    d["decc"] = nc.dram_tensor("decc", [128, 32], F32, kind="ExternalInput")
    d["epsc"] = nc.dram_tensor("epsc", [128, 32], F32, kind="ExternalInput")
    d["idf"] = nc.dram_tensor("idf", [128, 128], F32, kind="ExternalInput")
    d["y"] = nc.dram_tensor("y", [D, T], F32, kind="ExternalOutput")
    d["xA"] = nc.dram_tensor("xA", [D, T], F32)
    d["xB"] = nc.dram_tensor("xB", [D, T], F32)
    d["sw13"] = nc.dram_tensor("sw13", [NE * W13_E], BF16)
    d["sw2"] = nc.dram_tensor("sw2", [NE * W2_E], BF16)

    io = {}
    prev = d["x0"]
    for p in parts:
        io[p] = (prev, d["y"] if p == parts[-1] else
                 (d["xA"] if p == "ffn1" else d["xB"]))
        prev = io[p][1]

    with ExitStack() as st:
        def sb(name, shape, dt):
            return st.enter_context(nc.sbuf_tensor(name, shape, dt))
        gl = {}
        gl["identf"] = sb("identf", [128, 128], F32)
        gl["identb"] = sb("identb", [128, 128], BF16)
        gl["dec_s"] = sb("dec_s", [128, 32], F32)
        gl["eps_s"] = sb("eps_s", [128, 32], F32)
        gl["wg_s"] = sb("wg_s", [128, 64], F32)
        gl["cg8"] = sb("cg8", [128, 8], F32)
        gl["bq8"] = sb("bq8s", [128, 8], F32)
        gl["bo8"] = sb("bo8s", [128, 8], F32)
        gl["bkrow"] = sb("bkrow", [1, D], BF16)
        gl["bvrow"] = sb("bvrow", [1, D], BF16)
        gl["onesc_f"] = sb("onesc_f", [128, 1], F32)
        gl["onesc_b"] = sb("onesc_b", [128, 1], BF16)
        gl["onesr_f"] = sb("onesr_f", [33, 128], F32)
        gl["onesr_b"] = sb("onesr_b", [1, 128], BF16)
        gl["epsln"] = sb("epsln", [1, 1], F32)
        gl["rowtmp"] = sb("rowtmp", [1, 8], F32)

        sem_names = [
            "sC", "sV", "sM", "sPC", "sPP", "sPD", "sGE",
            "sX", "sSQ", "sST", "sGL", "sRW", "sTR", "sRC", "sGT", "sSG",
            "sGW", "sBC", "sBE", "sXH", "sGR", "sMW", "sWB",
            "sWUa", "sWUb", "sWD", "sWDR", "sM1", "sMU", "sSI", "sEV",
            "sMD", "sVD", "sVA", "sVO", "sXOa", "sXOb",
            "aW", "aXa", "aXb", "aSQ", "aST", "aRW", "aBC", "aBE", "aXH",
            "aKM", "aEV", "aMB", "aKV", "aQM", "aQE", "aNM", "aDN", "aCE",
            "aTR", "aCX", "aOM", "aXOa", "aXOb", "aOV", "aSVa", "aSVb",
        ]
        S = {k: Ctr(nc, st, k) for k in sem_names}

        with ExitStack() as stp:
            gl["pA"] = stp.enter_context(nc.psum_tensor("pA", [1, 512], F32))
            gl["pB"] = stp.enter_context(nc.psum_tensor("pB", [128, 512],
                                                        F32))
            emit_prelude(nc, gl, S, d)

        for p in parts:
            xin, xout = io[p]
            if p in ("ffn1", "ffn2"):
                with ExitStack() as stf:
                    g = alloc_ffn(nc, stf, "f1" if p == "ffn1" else "f2")
                    if p == "ffn1":
                        g["wstage"] = [stf.enter_context(nc.sbuf_tensor(
                            f"wstage{i}", [128, 2048], BF16))
                            for i in range(2)]
                        emit_prep(nc, g, S, d)
                    emit_ffn(nc, g, gl, S, d, xin, xout, p == "ffn1",
                             experts=_DBG.get("experts", NE),
                             quarters=_DBG.get("quarters", 4))
            else:
                with ExitStack() as sta:
                    emit_attn(nc, sta, gl, S, d, xin, xout)
        nc.all_engine_barrier()
    return nc


def emit_attn(nc, st, gl, S, d, xin, xout):
    (aW, aSQ, aST, aRW, aBC, aBE, aXH, aKM, aEV, aMB, aKV, aQM, aQE,
     aNM, aDN, aCE, aTR, aCX, aOM, aOV) = (
        S[k] for k in [
            "aW", "aSQ", "aST", "aRW", "aBC", "aBE", "aXH", "aKM",
            "aEV", "aMB", "aKV", "aQM", "aQE", "aNM", "aDN", "aCE", "aTR",
            "aCX", "aOM", "aOV"])

    def sba(name, shape, dt):
        return st.enter_context(nc.sbuf_tensor("at" + name, shape, dt))

    xha = [sba(f"xha{s}", [128, T], BF16) for s in range(8)]
    wpA = sba("wpA", [128, 8192], BF16)
    wpB = sba("wpB", [128, 8192], BF16)
    kvd = sba("kvd", [128, 1040], BF16)
    aRS = sba("aRS", [33, T], F32)

    for wsrc, wdst in ((d["wkt"], wpA), (d["wvt"], wpB)):
        aW.inc(nc.gpsimd.dma_start(
            bass.AP(wdst, 0, [[8192, 128], [1024, 8], [1, 1024]]),
            bass.AP(wsrc, 0, [[1024, 128], [131072, 8], [1, 1024]])), 16)

    phases = _DBG.get("attn_phases", "ABCDE")
    with ExitStack() as s1:
        fb0 = [s1.enter_context(nc.sbuf_tensor(f"atfb0{p}", [128, 2048], F32))
               for p in range(2)]
        fbs = [s1.enter_context(nc.sbuf_tensor(f"atfbs{p}", [128, 2048], F32))
               for p in range(2)]
        arM = s1.enter_context(nc.sbuf_tensor("atarM", [1, T], F32))
        arV = s1.enter_context(nc.sbuf_tensor("atarV", [1, T], F32))
        arT = s1.enter_context(nc.sbuf_tensor("atarT", [1, T], F32))
        pst = [s1.enter_context(nc.psum_tensor(f"atpst{c}", [33, 512], F32))
               for c in range(8)]
        ast_base = aST.n
        for k in range(16):
            s, half = k // 2, k % 2
            aXp = S["aXa"] if k % 2 == 0 else S["aXb"]
            if k >= 2:
                aST.wait(nc.sync, ast_base + k - 1)
            aXp.inc(nc.sync.dma_start(
                fb0[k % 2][:, :],
                bass.AP(xin, s * 128 * T + 2048 * half, [[T, 128], [1, 2048]])),
                16)
            aXp.wait(nc.scalar)
            if k >= 2:
                aST.wait(nc.scalar, ast_base + k - 1)
            aSQ.inc(nc.scalar.activation(fbs[k % 2][:, :], fb0[k % 2][:, :],
                                         AF.Square))
            aXp.wait(nc.tensor)
            aSQ.wait(nc.tensor)
            for c2 in range(4):
                c = 4 * half + c2
                nc.tensor.matmul(pst[c][0:1, :], gl["onesc_f"][:, :],
                                 fb0[k % 2][:, 512 * c2:512 * (c2 + 1)],
                                 start=(s == 0), stop=(s == 7))
                mm = nc.tensor.matmul(pst[c][32:33, :], gl["onesc_f"][:, :],
                                      fbs[k % 2][:, 512 * c2:512 * (c2 + 1)],
                                      start=(s == 0), stop=(s == 7))
            aST.inc(mm)
        aST.wait(nc.vector)
        for c in range(8):
            nc.vector.tensor_scalar(arM[0:1, 512 * c:512 * (c + 1)],
                                    pst[c][0:1, :], 1.0 / D, None, MUL)
            aRW.inc(nc.vector.tensor_scalar(arV[0:1, 512 * c:512 * (c + 1)],
                                            pst[c][32:33, :], 1.0 / D,
                                            None, MUL))
        aRW.wait(nc.vector)
        aRW.inc(nc.vector.tensor_tensor(arT[0:1, :], arM[0:1, :],
                                        arM[0:1, :], MUL))
        aRW.wait(nc.vector)
        aRW.inc(nc.vector.tensor_tensor(arV[0:1, :], arV[0:1, :],
                                        arT[0:1, :], SUB))
        aRW.wait(nc.scalar)
        aRW.inc(nc.scalar.activation(arV[0:1, :], arV[0:1, :], AF.Sqrt,
                                     bias=gl["epsln"][0:1, 0:1]))
        aRW.wait(nc.vector)
        aRW.inc(nc.vector.reciprocal(aRS[0:1, :], arV[0:1, :]))
        aRW.wait(nc.vector)
        aRW.inc(nc.vector.tensor_tensor(aRS[32:33, :], arM[0:1, :],
                                        aRS[0:1, :], MUL))
        nc.all_engine_barrier()

    if "B" not in phases:
        return
    with ExitStack() as s2:
        rbc2 = s2.enter_context(nc.sbuf_tensor("atrbc", [128, 1024], F32))
        mrbc2 = s2.enter_context(nc.sbuf_tensor("atmrbc", [128, 1024], F32))
        axo = [s2.enter_context(nc.sbuf_tensor(f"ataxo{p}", [128, 1024], F32))
               for p in range(2)]
        atmp = [s2.enter_context(nc.sbuf_tensor(f"atatmp{p}", [128, 1024],
                                                F32)) for p in range(2)]
        pb = [s2.enter_context(nc.psum_tensor(f"atpb{i}", [128, 512], F32))
              for i in range(4)]
        for c4 in range(4):
            bc_base = aBC.n
            for rnd in range(4):
                src = (aRS[0:1, 1024 * c4 + 512 * (rnd % 2):
                           1024 * c4 + 512 * (rnd % 2) + 512] if rnd < 2 else
                       aRS[32:33, 1024 * c4 + 512 * (rnd % 2):
                           1024 * c4 + 512 * (rnd % 2) + 512])
                if c4 > 0:
                    aBE.wait(nc.tensor, aBE.n - 3)
                ones_row = (gl["onesr_f"][0:1, :] if rnd < 2 else
                            gl["onesr_f"][32:33, :])
                aBC.inc(nc.tensor.matmul(pb[rnd][:, :], ones_row,
                                         src, start=True, stop=True))
                aBC.wait(nc.vector, bc_base + rnd + 1)
                if rnd == 0:
                    aXH.wait(nc.vector)
                dst = (rbc2 if rnd < 2 else mrbc2)[:, 512 * (rnd % 2):
                                                   512 * (rnd % 2) + 512]
                aBE.inc(nc.vector.tensor_copy(dst, pb[rnd][:, :]))
            for s in range(8):
                k = 8 * c4 + s
                aXp = S["aXa"] if k % 2 == 0 else S["aXb"]
                if k >= 2:
                    aXH.wait(nc.sync, aXH.n - 2)
                aXp.inc(nc.sync.dma_start(
                    axo[k % 2][:, :],
                    bass.AP(xin, s * 128 * T + 1024 * c4,
                            [[T, 128], [1, 1024]])), 16)
                aXp.wait(nc.vector)
                aBE.wait(nc.vector)
                aXH.inc(nc.vector.tensor_tensor(atmp[k % 2][:, :],
                                                axo[k % 2][:, :],
                                                rbc2[:, :], MUL))
                aXH.wait(nc.vector)
                aXH.inc(nc.vector.tensor_tensor(
                    xha[s][:, 1024 * c4:1024 * (c4 + 1)],
                    atmp[k % 2][:, :], mrbc2[:, :], SUB))
        nc.all_engine_barrier()

    if "C" not in phases:
        return
    with ExitStack() as s3:
        kf = s3.enter_context(nc.sbuf_tensor("atkf", [128, 1024], BF16))
        vf = s3.enter_context(nc.sbuf_tensor("atvf", [128, 1024], BF16))
        pk = [s3.enter_context(nc.psum_tensor(f"atpk{i}", [128, 512], F32))
              for i in range(2)]
        pv = [s3.enter_context(nc.psum_tensor(f"atpv{i}", [128, 512], F32))
              for i in range(2)]
        pkv = [s3.enter_context(nc.psum_tensor(f"atpkv{i}", [128, 512], F32))
               for i in range(2)]
        aW.wait(nc.tensor)
        aev_base, amb_base = aEV.n, aMB.n
        for j in range(32):
            for c in range(2):
                if j > 0:
                    aEV.wait(nc.tensor, aev_base + 4 * (j - 1) + c + 1)
                for s in range(8):
                    nc.tensor.matmul(
                        pk[c][:, :], xha[s][:, 128 * j:128 * (j + 1)],
                        wpA[:, 1024 * s + 512 * c:1024 * s + 512 * (c + 1)],
                        start=(s == 0), stop=False)
                aKM.inc(nc.tensor.matmul(
                    pk[c][:, :], gl["onesr_b"][0:1, :],
                    gl["bkrow"][0:1, 512 * c:512 * (c + 1)],
                    start=False, stop=True))
            for c in range(2):
                if j > 0:
                    aEV.wait(nc.tensor, aev_base + 4 * (j - 1) + c + 3)
                for s in range(8):
                    nc.tensor.matmul(
                        pv[c][:, :], xha[s][:, 128 * j:128 * (j + 1)],
                        wpB[:, 1024 * s + 512 * c:1024 * s + 512 * (c + 1)],
                        start=(s == 0), stop=False)
                aKM.inc(nc.tensor.matmul(
                    pv[c][:, :], gl["onesr_b"][0:1, :],
                    gl["bvrow"][0:1, 512 * c:512 * (c + 1)],
                    start=False, stop=True))
            for c in range(2):
                aKM.wait(nc.scalar, aKM.n - 4 + c + 1)
                if j > 0:
                    aMB.wait(nc.scalar, amb_base + j)
                aEV.inc(nc.scalar.activation(kf[:, 512 * c:512 * (c + 1)],
                                             pk[c][:, :], AF.Relu,
                                             scale=gl["dec_s"][:, j:j + 1]))
            for c in range(2):
                aKM.wait(nc.scalar, aKM.n - 2 + c + 1)
                aEV.inc(nc.scalar.copy(vf[:, 512 * c:512 * (c + 1)],
                                       pv[c][:, :]))
            aEV.wait(nc.tensor)
            for h in range(NH):
                pg = pkv[h // 8]
                prow = 64 * (h % 2)
                fcol = 65 * ((h // 2) % 4)
                nc.tensor.matmul(pg[prow:prow + 64, fcol:fcol + 64],
                                 kf[:, 64 * h:64 * (h + 1)],
                                 vf[:, 64 * h:64 * (h + 1)],
                                 start=(j == 0 and h % 8 < 2), stop=False,
                                 skip_group_check=True)
                mm = nc.tensor.matmul(pg[prow:prow + 64, fcol + 64:fcol + 65],
                                      kf[:, 64 * h:64 * (h + 1)],
                                      gl["onesc_b"][:, :], start=False,
                                      stop=(j == 31 and h % 8 >= 6),
                                      skip_group_check=True)
            aMB.inc(mm)
        aMB.wait(nc.vector)
        aKV.inc(nc.vector.memset(kvd[:, :], 0.0))
        aKV.wait(nc.vector)
        for h in range(NH):
            prow = 64 * (h % 2)
            inst = nc.vector.tensor_copy(
                kvd[prow:prow + 64,
                    130 * (h // 2) + 65 * (h % 2):
                    130 * (h // 2) + 65 * (h % 2) + 65],
                pkv[h // 8][prow:prow + 64,
                            65 * ((h // 2) % 4):65 * ((h // 2) % 4) + 65])
            if h == NH - 1:
                aKV.inc(inst)
        nc.all_engine_barrier()

    if "D" not in phases:
        return
    sDE = st.enter_context(ExitStack())
    ctx = [sDE.enter_context(nc.sbuf_tensor(f"atctx{s}", [128, T], BF16))
           for s in range(8)]
    aW.inc(nc.gpsimd.dma_start(
        bass.AP(wpA, 0, [[8192, 128], [1024, 8], [1, 1024]]),
        bass.AP(d["wqt"], 0, [[1024, 128], [131072, 8], [1, 1024]])), 16)

    with ExitStack() as s4:
        qf = [s4.enter_context(nc.sbuf_tensor(f"atqf{i}", [128, 512], BF16))
              for i in range(8)]
        ctok = [s4.enter_context(nc.sbuf_tensor(f"atct{p}", [128, 1024],
                                                BF16)) for p in range(2)]
        dnb = s4.enter_context(nc.sbuf_tensor("atdnb", [128, 16], F32))
        rcb = s4.enter_context(nc.sbuf_tensor("atrcb", [128, 16], F32))
        pq = [s4.enter_context(nc.psum_tensor(f"atpq{i}", [128, 512], F32))
              for i in range(2)]
        pn = [s4.enter_context(nc.psum_tensor(f"atpn{i}", [128, 260], F32))
              for i in range(4)]
        pbt = [s4.enter_context(nc.psum_tensor(f"atpbt{i}", [128, 512],
                                               F32)) for i in range(2)]
        aW.wait(nc.tensor)
        aqe_base = aQE.n
        for ch in range(8):
            for isl in range(8):
                kq = 8 * ch + isl
                if kq >= 2:
                    aQE.wait(nc.tensor, aqe_base + kq - 1)
                for s in range(8):
                    mm = nc.tensor.matmul(
                        pq[isl % 2][:, :],
                        wpA[:, 1024 * s + 128 * isl:1024 * s + 128 * (isl + 1)],
                        xha[s][:, 512 * ch:512 * (ch + 1)],
                        start=(s == 0), stop=(s == 7))
                aQM.inc(mm)
                aQM.wait(nc.scalar)
                aNM.wait(nc.scalar, aNM.n)
                aQE.inc(nc.scalar.activation(qf[isl][:, :], pq[isl % 2][:, :],
                                             AF.Relu,
                                             bias=gl["bq8"][:, isl:isl + 1]))
            aQE.wait(nc.tensor, aqe_base + 8 * (ch + 1))
            dsub = _DBG.get("d_sub", 4)
            _rank = {1: 1, 2: 2, 3: 3, 33: 4, 34: 5, 35: 6, 4: 7}[dsub]
            for jj in range(4 if _rank >= 2 else 0):
                tg = 4 * ch + jj
                if tg > 0:
                    if _rank >= 3:
                        aCE.wait(nc.tensor, 16 * tg)
                    aDN.wait(nc.tensor, aDN.n)
                for isl in range(8):
                    gi = isl // 2
                    mm = nc.tensor.matmul(
                        pn[gi][:, 130 * (isl % 2):130 * (isl % 2) + 130],
                        qf[isl][:, 128 * jj:128 * (jj + 1)],
                        kvd[:, 130 * isl:130 * (isl + 1)],
                        start=True, stop=True)
                    if isl % 2 == 1:
                        aNM.inc(mm)
                aNM.wait(nc.vector)
                aKV.wait(nc.vector)
                for gi in range(4):
                    inst = nc.vector.tensor_scalar(
                        dnb[:, 4 * gi:4 * gi + 4],
                        bass.AP(pn[gi], 64, [[260, 128], [65, 4]]),
                        gl["eps_s"][:, tg:tg + 1], None, ADD)
                    if gi == 3:
                        aDN.inc(inst)
                aDN.wait(nc.vector)
                aDN.inc(nc.vector.reciprocal(rcb[:, :], dnb[:, :]))
                if _rank < 3:
                    continue
                aDN.wait(nc.vector)
                aDN.wait(nc.scalar)
                for h in range(NH):
                    gi = h // 4
                    src = pn[gi][:, 65 * (h % 4):65 * (h % 4) + 64]
                    dst = ctok[jj % 2][:, 64 * h:64 * (h + 1)]
                    if h % 2 == 0:
                        inst = nc.scalar.activation(dst, src, AF.Copy,
                                                    scale=rcb[:, h:h + 1])
                    else:
                        inst = nc.vector.tensor_scalar(dst, src,
                                                       rcb[:, h:h + 1],
                                                       None, MUL)
                    aCE.inc(inst)
                if _rank < 4:
                    continue
                aCE.wait(nc.tensor, 16 * (tg + 1))
                if _rank >= 6:
                    aCX.wait(nc.tensor, 8 * tg)
                for s in range(8 if _rank >= 5 else 0):
                    aTR.inc(nc.tensor.matmul(
                        pbt[s // 4][:, 128 * (s % 4):128 * (s % 4) + 128],
                        ctok[jj % 2][:, 128 * s:128 * (s + 1)],
                        gl["identb"][:, :], start=True, stop=True))
                if _rank >= 6:
                    aTR.wait(nc.vector)
                    for s in range(8):
                        aCX.inc(nc.vector.tensor_copy(
                            ctx[s][:, 128 * tg:128 * (tg + 1)],
                            pbt[s // 4][:, 128 * (s % 4):128 * (s % 4) + 128]))
        nc.all_engine_barrier()

    if "E" not in phases:
        return
    aW.inc(nc.gpsimd.dma_start(
        bass.AP(wpB, 0, [[8192, 128], [1024, 8], [1, 1024]]),
        bass.AP(d["wot"], 0, [[1024, 128], [131072, 8], [1, 1024]])), 16)

    with ExitStack() as s5:
        xo = [s5.enter_context(nc.sbuf_tensor(f"atxo{p}", [128, 512], F32))
              for p in range(2)]
        xn = [s5.enter_context(nc.sbuf_tensor(f"atxn{p}", [128, 512], F32))
              for p in range(2)]
        po = [s5.enter_context(nc.psum_tensor(f"atpo{i}", [128, 512], F32))
              for i in range(2)]
        aW.wait(nc.tensor)
        for isl in range(8):
            for ch in range(8):
                k = 8 * isl + ch
                par = k % 2
                aXOp = S["aXOa"] if par == 0 else S["aXOb"]
                aSVp = S["aSVa"] if par == 0 else S["aSVb"]
                if k >= 2:
                    aOV.wait(nc.sync, 2 * (k - 1))
                aXOp.inc(nc.sync.dma_start(
                    xo[par][:, :],
                    bass.AP(xin, isl * 128 * T + 512 * ch,
                            [[T, 128], [1, 512]])), 16)
                if k >= 2:
                    aOV.wait(nc.tensor, 2 * (k - 1))
                for s in range(8):
                    mm = nc.tensor.matmul(
                        po[par][:, :],
                        wpB[:, 1024 * s + 128 * isl:1024 * s + 128 * (isl + 1)],
                        ctx[s][:, 512 * ch:512 * (ch + 1)],
                        start=(s == 0), stop=(s == 7))
                aOM.inc(mm)
                aOM.wait(nc.vector)
                aXOp.wait(nc.vector)
                if k >= 2:
                    aSVp.wait(nc.vector)
                aOV.inc(nc.vector.tensor_scalar(xn[par][:, :], po[par][:, :],
                                                gl["bo8"][:, isl:isl + 1],
                                                None, ADD))
                aOV.wait(nc.vector)
                aOV.inc(nc.vector.tensor_tensor(xn[par][:, :], xn[par][:, :],
                                                xo[par][:, :], ADD))
                aOV.wait(nc.sync)
                aSVp.inc(nc.sync.dma_start(
                    bass.AP(xout, isl * 128 * T + 512 * ch,
                            [[T, 128], [1, 512]]), xn[par][:, :]), 16)
        S["aSVa"].wait(nc.sync)
        S["aSVb"].wait(nc.sync)
        nc.all_engine_barrier()


_NC_CACHE = {}
_DBG = {"three_launch": True}


def _host_layouts(L0_attn_W_q, L0_attn_b_q, L0_attn_W_k, L0_attn_b_k,
                  L0_attn_W_v, L0_attn_b_v, L0_attn_W_o, L0_attn_b_o,
                  L0_ffn_Wg, L0_ffn_Ws1, L0_ffn_Ws3, L0_ffn_Ws2,
                  L0_ffn_We1, L0_ffn_We3, L0_ffn_We2):
    f32 = lambda a: np.asarray(a, dtype=np.float32)

    def up_reorder(We, Ws):
        arr = np.concatenate([f32(We), f32(Ws)[None]], axis=0)
        arr = arr.reshape(9, 8, 128, 16, 128).transpose(0, 3, 2, 1, 4)
        return arr.reshape(9, 16, 128, 1024)

    a1 = up_reorder(L0_ffn_We1, L0_ffn_Ws1)
    a3 = up_reorder(L0_ffn_We3, L0_ffn_Ws3)
    w13r = np.ascontiguousarray(np.concatenate([a1, a3], axis=3).reshape(-1))
    a2 = np.concatenate([f32(L0_ffn_We2), f32(L0_ffn_Ws2)[None]], axis=0)
    w2r = np.ascontiguousarray(
        a2.reshape(9, 16, 128, 1024).transpose(0, 2, 1, 3).reshape(-1))

    n = np.arange(T, dtype=np.float64)
    decc = np.ascontiguousarray(
        np.exp(-n / T).astype(np.float32).reshape(32, 128).T)
    epsc = np.ascontiguousarray(
        (1e-6 * np.exp(n / T)).astype(np.float32).reshape(32, 128).T)

    col8 = lambda b: np.ascontiguousarray(f32(b).reshape(8, 128).T)
    return dict(
        wqt=np.ascontiguousarray(f32(L0_attn_W_q).T),
        wkt=np.ascontiguousarray(f32(L0_attn_W_k).T),
        wvt=np.ascontiguousarray(f32(L0_attn_W_v).T),
        wot=np.ascontiguousarray(f32(L0_attn_W_o).T),
        bq8=col8(L0_attn_b_q), bo8=col8(L0_attn_b_o),
        bkr=np.ascontiguousarray(f32(L0_attn_b_k).reshape(1, D)),
        bvr=np.ascontiguousarray(f32(L0_attn_b_v).reshape(1, D)),
        wg8=np.ascontiguousarray(
            f32(L0_ffn_Wg).reshape(8, 128, 8).transpose(1, 0, 2)
            .reshape(128, 64)),
        w13r=w13r, w2r=w2r, decc=decc, epsc=epsc,
        idf=np.eye(128, dtype=np.float32),
    )


def kernel(L0_attn_x, L0_attn_W_q, L0_attn_b_q, L0_attn_W_k, L0_attn_b_k,
           L0_attn_W_v, L0_attn_b_v, L0_attn_W_o, L0_attn_b_o,
           L0_ffn_Wg, L0_ffn_Ws1, L0_ffn_Ws3, L0_ffn_Ws2,
           L0_ffn_We1, L0_ffn_We3, L0_ffn_We2):
    x = np.asarray(L0_attn_x, dtype=np.float32)
    B = x.shape[0]
    common = _host_layouts(
        L0_attn_W_q, L0_attn_b_q, L0_attn_W_k, L0_attn_b_k, L0_attn_W_v,
        L0_attn_b_v, L0_attn_W_o, L0_attn_b_o, L0_ffn_Wg, L0_ffn_Ws1,
        L0_ffn_Ws3, L0_ffn_Ws2, L0_ffn_We1, L0_ffn_We3, L0_ffn_We2)
    if _DBG.get("three_launch"):
        cur = [np.ascontiguousarray(x[b].T) for b in range(B)]
        for part in ("ffn1", "attn", "ffn2"):
            if part not in _NC_CACHE:
                _NC_CACHE[part] = build_nc(parts=(part,))
            in_maps = [dict(common, x0=cur[b]) for b in range(B)]
            res = run_bass_kernel_spmd(_NC_CACHE[part], in_maps,
                                       core_ids=list(range(B)))
            cur = [np.ascontiguousarray(np.asarray(r["y"], np.float32))
                   for r in res.results]
        return np.stack([c.T for c in cur])
    if "full" not in _NC_CACHE:
        _NC_CACHE["full"] = build_nc()
    nc = _NC_CACHE["full"]
    in_maps = [dict(common, x0=np.ascontiguousarray(x[b].T))
               for b in range(B)]
    res = run_bass_kernel_spmd(nc, in_maps, core_ids=list(range(B)))
    out = np.stack([np.asarray(r["y"], np.float32).T for r in res.results])
    return out
